# version 02
# baseline (speedup 1.0000x reference)
import math
from contextlib import ExitStack

import numpy as np
import ml_dtypes

import concourse.bass as bass
import concourse.mybir as mybir
from concourse.bass_utils import run_bass_kernel_spmd

F32 = mybir.dt.float32
BF16 = mybir.dt.bfloat16
I32 = mybir.dt.int32
AF = mybir.ActivationFunctionType
ALU = mybir.AluOpType
AX = mybir.AxisListType
BFD = ml_dtypes.bfloat16

D = 1024
S = 2048
NCH = 8
HD = 32
NPAIR = 16
NDUO = 8
QT = 512
NQT = S // QT
NKC = S // 128
INNER = 4096
NIC = INNER // 128
LAMBDA_INIT = 0.8 - 0.6 * math.exp(-0.3 * 12)
SCAL = 1.0 / math.sqrt(32.0)
EPS = 1e-5
THETA = 1e-4


def _host_constants():
    c = {}
    i_idx = (np.arange(128) % 32) // 2
    freq = (1.0 / THETA) ** (2.0 * i_idx / 32.0)
    ang = np.arange(S, dtype=np.float64)[None, :] * freq[:, None]
    c["costab"] = np.cos(ang).astype(np.float32)
    c["sintab"] = np.sin(ang).astype(np.float32)
    P = np.zeros((128, 128), np.float32)
    for b in range(0, 128, 32):
        for i in range(16):
            P[b + 2 * i, b + 2 * i + 1] = -1.0
            P[b + 2 * i + 1, b + 2 * i] = 1.0
    c["ptab"] = P.T.astype(BFD)
    c["id128b"] = np.eye(128, dtype=BFD)
    c["id128f"] = np.eye(128, dtype=np.float32)
    c["negid2f"] = (-np.eye(2, dtype=np.float32))
    c["onescolb"] = np.ones((128, 1), BFD)
    c["onescolf"] = np.ones((128, 1), np.float32)
    c["onesrowf"] = np.ones((1, 128), np.float32)
    c["onesrowb"] = np.ones((1, 128), BFD)
    selE = np.zeros((128, 2), np.float32)
    selE[0:32, 0] = 1.0
    selE[64:96, 1] = 1.0
    selO = np.zeros((128, 2), np.float32)
    selO[32:64, 0] = 1.0
    selO[96:128, 1] = 1.0
    c["selE"] = selE.astype(BFD)
    c["selO"] = selO.astype(BFD)
    return c


class Prog:

    def __init__(self, nc, es):
        self.c = {}
        self.h = {}
        self.lw = {}
        for n in ("dS", "dG", "pT", "pV", "pS"):
            self.h[n] = es.enter_context(nc.semaphore(n))
            self.c[n] = 0

    def inc(self, instr, sem, k=1):
        instr.then_inc(self.h[sem], k)
        self.c[sem] += k
        return instr

    def wait(self, eng, lbl, sem, val=None):
        v = self.c[sem] if val is None else val
        if v <= self.lw.get((lbl, sem), 0):
            return
        eng.wait_ge(self.h[sem], v)
        self.lw[(lbl, sem)] = v

    def barrier(self, engs):
        for eng, lbl in engs:
            for sem in ("dS", "dG", "pT", "pV", "pS"):
                self.wait(eng, lbl, sem)


def build_nc():
    nc = bass.Bass("TRN2", target_bir_lowering=False)
    dr = {}
    dr["x"] = nc.dram_tensor("x", [D, S], F32, kind="ExternalInput")
    for n in ("wq", "wk", "wv", "wo"):
        dr[n] = nc.dram_tensor(n, [D, D], F32, kind="ExternalInput")
    dr["wmod"] = nc.dram_tensor("wmod", [D, 6 * D], F32, kind="ExternalInput")
    dr["wg"] = nc.dram_tensor("wg", [D, INNER], F32, kind="ExternalInput")
    dr["wu"] = nc.dram_tensor("wu", [D, INNER], F32, kind="ExternalInput")
    dr["wd"] = nc.dram_tensor("wd", [INNER, D], F32, kind="ExternalInput")
    dr["cvec"] = nc.dram_tensor("cvec", [D, 1], F32, kind="ExternalInput")
    for n in ("lq1", "lk1", "lq2", "lk2"):
        dr[n] = nc.dram_tensor(n, [1, HD], F32, kind="ExternalInput")
    dr["subg"] = nc.dram_tensor("subg", [128, 1], F32, kind="ExternalInput")
    for n, sh, dt in (
        ("costab", [128, S], F32), ("sintab", [128, S], F32),
        ("ptab", [128, 128], BF16), ("id128b", [128, 128], BF16),
        ("id128f", [128, 128], F32), ("negid2f", [2, 2], F32),
        ("onescolb", [128, 1], BF16), ("onescolf", [128, 1], F32),
        ("onesrowf", [1, 128], F32), ("onesrowb", [1, 128], BF16),
        ("selE", [128, 2], BF16), ("selO", [128, 2], BF16),
    ):
        dr[n] = nc.dram_tensor(n, sh, dt, kind="ExternalInput")
    dr["x2scr"] = nc.dram_tensor("x2scr", [D, S], F32)
    dr["wgq"] = nc.dram_tensor("wgq", [D, INNER], BF16)
    dr["wuq"] = nc.dram_tensor("wuq", [D, INNER], BF16)
    dr["wdq"] = nc.dram_tensor("wdq", [INNER, D], BF16)
    dr["out"] = nc.dram_tensor("out", [D, S], F32, kind="ExternalOutput")

    with ExitStack() as es:
        _emit(nc, dr, es)
    return nc



STOP_AFTER = "full"


def _emit(nc, dr, es):
    T, V, SC, G, SY = nc.tensor, nc.vector, nc.scalar, nc.gpsimd, nc.sync
    ENGS = [(T, "t"), (V, "v"), (SC, "s"), (G, "g"), (SY, "y")]
    p = Prog(nc, es)

    def sb(name, shape, dt):
        return es.enter_context(nc.sbuf_tensor(name, shape, dt))

    ap = lambda n: dr[n].ap()

    pb0 = es.enter_context(nc.psum_tensor("pb0", [128, 1024], F32))
    pb1 = es.enter_context(nc.psum_tensor("pb1", [128, 1024], F32))
    pz = es.enter_context(nc.psum_tensor("pz", [128, 512], F32))
    pv0 = es.enter_context(nc.psum_tensor("pv0", [128, 512], F32))
    pv1 = es.enter_context(nc.psum_tensor("pv1", [128, 512], F32))
    pm = es.enter_context(nc.psum_tensor("pm", [128, 512], F32))

    c_ptab = sb("c_ptab", [128, 128], BF16)
    c_idb = sb("c_idb", [128, 128], BF16)
    c_idf = sb("c_idf", [128, 128], F32)
    c_nid2 = sb("c_nid2", [2, 2], F32)
    c_ocb = sb("c_ocb", [128, 1], BF16)
    c_ocf = sb("c_ocf", [128, 1], F32)
    c_orf = sb("c_orf", [1, 128], F32)
    c_orb = sb("c_orb", [1, 128], BF16)
    c_selE = sb("c_selE", [128, 2], BF16)
    c_selO = sb("c_selO", [128, 2], BF16)
    modc = sb("modc", [128, 48], F32)
    capl1 = sb("capl1", [128, 16], F32)
    gacol = sb("gacol", [128, 16], F32)
    gpcol = sb("gpcol", [128, 1], F32)
    oneslam = sb("oneslam", [128, 64], F32)
    lamcol = sb("lamcol", [128, 1], F32)
    lam11 = sb("lam11", [1, 1], F32)
    subgs = sb("subgs", [128, 1], F32)
    swcol = sb("swcol", [128, 3], F32)

    for dst, src in [(c_ptab, "ptab"), (c_idb, "id128b"), (c_idf, "id128f"),
                     (c_nid2, "negid2f"), (c_ocb, "onescolb"), (c_ocf, "onescolf"),
                     (c_orf, "onesrowf"), (c_orb, "onesrowb"), (c_selE, "selE"),
                     (c_selO, "selO"), (subgs, "subg")]:
        p.wait(SY, "y", "dS")
        p.inc(SY.dma_start(dst[:], ap(src)), "dS", 16)

    with ExitStack() as e1:
        cvecs = e1.enter_context(nc.sbuf_tensor("cvecs", [128, 8], F32))
        siluc = e1.enter_context(nc.sbuf_tensor("siluc", [128, 8], BF16))
        wmodb = e1.enter_context(nc.sbuf_tensor("wmodb", [128, 512], BF16))
        modrow = e1.enter_context(nc.sbuf_tensor("modrow", [1, 6 * D], F32))
        lrow = e1.enter_context(nc.sbuf_tensor("lrow", [1, HD], F32))
        l11b = e1.enter_context(nc.sbuf_tensor("l11b", [1, 1], F32))
        lqs = [e1.enter_context(nc.sbuf_tensor(f"lqs{i}", [1, HD], F32))
               for i in range(4)]

        for j in range(8):
            p.wait(SY, "y", "dS")
            p.inc(SY.dma_start(cvecs[:, j:j + 1],
                               dr["cvec"].ap()[128 * j:128 * (j + 1), :]), "dS", 16)
        for t, n in zip(lqs, ("lq1", "lk1", "lq2", "lk2")):
            p.wait(SY, "y", "dS")
            p.inc(SY.dma_start(t[:], ap(n)), "dS", 16)
        p.wait(SC, "s", "dS")
        p.inc(SC.activation(siluc[:], cvecs[:], AF.Silu), "pS")

        p.wait(V, "v", "dS")
        p.inc(V.tensor_mul(lrow[:], lqs[0][:], lqs[1][:]), "pV")
        p.wait(V, "v", "pV")
        p.inc(V.reduce_sum(lam11[:], lrow[:], axis=AX.X), "pV")
        p.wait(SC, "s", "pV")
        p.inc(SC.activation(l11b[:], lam11[:], AF.Exp), "pS")
        p.wait(V, "v", "pS")
        p.inc(V.tensor_mul(lrow[:], lqs[2][:], lqs[3][:]), "pV")
        p.wait(V, "v", "pV")
        p.inc(V.reduce_sum(lam11[:], lrow[:], axis=AX.X), "pV")
        p.wait(SC, "s", "pV")
        p.inc(SC.activation(lam11[:], lam11[:], AF.Exp), "pS")
        p.wait(V, "v", "pS")
        p.inc(V.tensor_sub(lam11[:], l11b[:], lam11[:]), "pV")
        p.wait(V, "v", "pV")
        p.inc(V.tensor_scalar_add(lam11[:], lam11[:], float(LAMBDA_INIT)), "pV")


        for sl in range(12):
            for kc in range(NCH):
                p.wait(G, "g", "pT")
                p.wait(G, "g", "dG")
                p.inc(G.dma_start(wmodb[:], dr["wmod"].ap()[128 * kc:128 * (kc + 1),
                                                            512 * sl:512 * (sl + 1)]),
                      "dG", 16)
                p.wait(T, "t", "dG")
                p.wait(T, "t", "pS")
                p.inc(T.matmul(pz[0:1, :], siluc[:, kc:kc + 1], wmodb[:],
                               start=(kc == 0), stop=(kc == NCH - 1)), "pT")
            p.wait(SC, "s", "pT")
            p.inc(SC.copy(modrow[0:1, 512 * sl:512 * (sl + 1)], pz[0:1, :]), "pS")
        p.wait(T, "t", "pS")
        for j in range(48):
            p.inc(T.matmul(pm[:, j:j + 1], modrow[0:1, 128 * j:128 * (j + 1)],
                           c_idf[0:1, 0:1], start=True, stop=True), "pT")
        p.wait(SC, "s", "pT")
        p.inc(SC.copy(modc[:], pm[:, 0:48]), "pS")

    p.wait(T, "t", "pS")
    p.wait(T, "t", "pV")
    p.inc(T.matmul(pz[:, 0:1], c_orf[:], lam11[:], start=True, stop=True), "pT")
    p.wait(SC, "s", "pT")
    p.inc(SC.copy(lamcol[:], pz[:, 0:1]), "pS")
    p.wait(V, "v", "pS")
    p.inc(V.memset(oneslam[:], 1.0), "pV")
    p.wait(V, "v", "pV")
    p.inc(V.tensor_scalar(oneslam[32:64, :], oneslam[32:64, :], lamcol[32:64, 0:1],
                          None, ALU.mult), "pV")
    p.inc(V.tensor_scalar(oneslam[96:128, :], oneslam[96:128, :], lamcol[96:128, 0:1],
                          None, ALU.mult), "pV")

    p.wait(V, "v", "pS")
    for j in range(8):
        p.inc(V.tensor_scalar_add(capl1[:, j:j + 1], modc[:, 8 + j:9 + j], 1.0), "pV")
        p.inc(V.tensor_scalar_add(capl1[:, 8 + j:9 + j], modc[:, 32 + j:33 + j], 1.0),
              "pV")
    with ExitStack() as e3:
        tneg = e3.enter_context(nc.sbuf_tensor("tneg", [128, 1], F32))
        tabs = e3.enter_context(nc.sbuf_tensor("tabs", [128, 1], F32))
        tsgn = e3.enter_context(nc.sbuf_tensor("tsgn", [128, 1], F32))
        for j in range(16):
            col = 16 + j if j < 8 else 40 + (j - 8)
            src = modc[:, col:col + 1]
            p.wait(V, "v", "pV")
            p.inc(V.tensor_scalar_mul(tneg[:], src, -1.0), "pV")
            p.wait(V, "v", "pV")
            p.inc(V.tensor_max(tabs[:], tneg[:], src), "pV")
            p.wait(V, "v", "pV")
            p.inc(V.tensor_scalar_max(tabs[:], tabs[:], 0.25), "pV")
            p.inc(V.tensor_scalar(tsgn[:], src, 0.0, 2.0, ALU.is_ge, ALU.mult), "pV")
            p.wait(V, "v", "pV")
            p.inc(V.tensor_scalar_sub(tsgn[:], tsgn[:], 1.0), "pV")
            p.wait(V, "v", "pV")
            p.inc(V.tensor_mul(gacol[:, j:j + 1], tabs[:], tsgn[:]), "pV")
    p.wait(V, "v", "dS")
    p.inc(V.tensor_scalar_mul(gpcol[:], subgs[:], float(1.0 - LAMBDA_INIT)), "pV")

    if STOP_AFTER == "p2":
        G.wait_ge(p.h["dG"], p.c["dG"])
        p.barrier(ENGS)
        return
    with ExitStack() as em:
        qm = lambda n, s_, d_: em.enter_context(nc.sbuf_tensor(n, s_, d_))
        qf = [qm(f"qf{i}", [128, S], BF16) for i in range(NCH)]
        kf = [qm(f"kf{i}", [128, S], BF16) for i in range(NCH)]
        vt = [qm(f"vt{i}", [128, D], BF16) for i in range(NKC)]
        mT = [qm(f"mT{d}", [128, 32], F32) for d in range(NDUO)]
        lT = [qm(f"lT{d}", [128, 32], BF16) for d in range(NDUO)]
        lTf = [qm(f"lTf{d}", [128, 32], F32) for d in range(NDUO)]

        with ExitStack() as eq:
            q = lambda n, s_, d_: eq.enter_context(nc.sbuf_tensor(n, s_, d_))
            costab = q("s_costab", [128, S], F32)
            sintab = q("s_sintab", [128, S], F32)
            hch = [q(f"h{i}", [128, S], BF16) for i in range(NCH)]
            wbf = q("wbf", [128, 8 * 1024], BF16)

            p.wait(SY, "y", "dS")
            p.inc(SY.dma_start(costab[:], ap("costab")), "dS", 16)
            p.wait(SY, "y", "dS")
            p.inc(SY.dma_start(sintab[:], ap("sintab")), "dS", 16)

            with ExitStack() as el:
                ql = lambda n, s_, d_: el.enter_context(nc.sbuf_tensor(n, s_, d_))
                xcb = ql("xcb", [128, S], F32)
                xb16 = ql("xb16", [128, QT], BF16)
                xsq = ql("xsq", [128, QT], BF16)
                murow = ql("murow", [1, 512], F32)
                exrow = ql("exrow", [1, 512], F32)
                rarow = ql("rarow", [1, 512], F32)
                rbrow = ql("rbrow", [1, 512], F32)
                rAb = ql("rAb", [128, S], F32)
                rBb = ql("rBb", [128, S], F32)

                bankx = [pb0[:, 0:512], pb0[:, 512:1024], pb1[:, 0:512],
                         pb1[:, 512:1024]]
                bankq = [pz[:, :], pv0[:, :], pv1[:, :], pm[:, :]]
                sx = [b[0:1, :] for b in bankx]
                sq_ = [b[0:1, :] for b in bankq]
                for ci in range(NCH):
                    p.wait(SY, "y", "pV")
                    p.wait(SY, "y", "dS")
                    p.inc(SY.dma_start(xcb[:],
                                       dr["x"].ap()[128 * ci:128 * (ci + 1), :]),
                          "dS", 16)
                    p.wait(V, "v", "dS")
                    for ts in range(NQT):
                        sl_ = slice(QT * ts, QT * (ts + 1))
                        p.wait(V, "v", "pT")
                        p.inc(V.tensor_copy(xb16[:], xcb[:, sl_]), "pV")
                        p.inc(V.tensor_tensor(xsq[:], xcb[:, sl_], xcb[:, sl_],
                                              ALU.mult), "pV")
                        p.wait(T, "t", "pV")
                        T.matmul(sx[ts], c_ocb[:], xb16[:], start=(ci == 0),
                                 stop=(ci == NCH - 1))
                        p.inc(T.matmul(sq_[ts], c_ocb[:], xsq[:], start=(ci == 0),
                                       stop=(ci == NCH - 1)), "pT")
                for ts in range(NQT):
                    sl_ = slice(QT * ts, QT * (ts + 1))
                    p.wait(SC, "s", "pT")
                    p.wait(SC, "s", "pV")
                    SC.copy(murow[:], sx[ts])
                    p.inc(SC.copy(exrow[:], sq_[ts]), "pS")
                    p.wait(V, "v", "pS")
                    p.inc(V.tensor_scalar_mul(murow[:], murow[:], 1.0 / D), "pV")
                    p.inc(V.tensor_scalar_mul(exrow[:], exrow[:], 1.0 / D), "pV")
                    p.wait(V, "v", "pV")
                    p.inc(V.tensor_tensor(rarow[:], murow[:], murow[:], ALU.mult),
                          "pV")
                    p.wait(V, "v", "pV")
                    p.inc(V.tensor_sub(rarow[:], exrow[:], rarow[:]), "pV")
                    p.wait(V, "v", "pV")
                    p.inc(V.tensor_scalar_add(rarow[:], rarow[:], EPS), "pV")
                    p.wait(V, "v", "pV")
                    p.inc(V.reciprocal(rarow[:], rarow[:]), "pV")
                    p.wait(SC, "s", "pV")
                    p.inc(SC.activation(rarow[:], rarow[:], AF.Sqrt), "pS")
                    p.wait(V, "v", "pS")
                    p.inc(V.scalar_tensor_tensor(rbrow[:], murow[:], -1.0,
                                                 rarow[:], ALU.mult, ALU.mult), "pV")
                    p.wait(T, "t", "pV")
                    p.wait(T, "t", "pS")
                    p.inc(T.matmul(bankx[ts], c_orf[:], rarow[:],
                                   start=True, stop=True), "pT")
                    p.wait(SC, "s", "pT")
                    p.inc(SC.copy(rAb[:, sl_], bankx[ts]), "pS")
                    p.wait(T, "t", "pS")
                    p.inc(T.matmul(bankq[ts], c_orf[:], rbrow[:],
                                   start=True, stop=True), "pT")
                    p.wait(SC, "s", "pT")
                    p.inc(SC.copy(rBb[:, sl_], bankq[ts]), "pS")
                with ExitStack() as eh:
                    t1f = eh.enter_context(nc.sbuf_tensor("t1f", [128, QT], F32))
                    t2f = eh.enter_context(nc.sbuf_tensor("t2f", [128, QT], F32))
                    for ci in range(NCH):
                        p.wait(SY, "y", "pV")
                        p.wait(SY, "y", "pS")
                        p.wait(SY, "y", "dS")
                        p.inc(SY.dma_start(xcb[:],
                                           dr["x"].ap()[128 * ci:128 * (ci + 1), :]),
                              "dS", 16)
                        p.wait(V, "v", "dS")
                        for ts in range(NQT):
                            sl_ = slice(QT * ts, QT * (ts + 1))
                            p.wait(V, "v", "pS")
                            p.wait(V, "v", "pV")
                            p.inc(V.tensor_tensor(t1f[:], xcb[:, sl_], rAb[:, sl_],
                                                  ALU.mult), "pV")
                            p.wait(V, "v", "pV")
                            p.inc(V.tensor_tensor(t2f[:], t1f[:], rBb[:, sl_],
                                                  ALU.add), "pV")
                            p.wait(SC, "s", "pV")
                            p.inc(SC.activation(hch[ci][:, sl_], t2f[:],
                                                AF.Identity,
                                                scale=capl1[:, ci:ci + 1],
                                                bias=modc[:, ci:ci + 1]), "pS")

            with ExitStack() as er:
                qraw = er.enter_context(nc.sbuf_tensor("qraw", [128, QT], BF16))
                ptmp = er.enter_context(nc.sbuf_tensor("ptmp", [128, QT], BF16))
                t1f = er.enter_context(nc.sbuf_tensor("t1r", [128, QT], F32))
                t2f = er.enter_context(nc.sbuf_tensor("t2r", [128, QT], F32))
                for which, wname, dst in (("q", "wq", qf), ("k", "wk", kf)):
                    p.wait(G, "g", "pT")
                    for kc in range(NCH):
                        p.wait(G, "g", "dG")
                        p.inc(G.dma_start(wbf[:, 1024 * kc:1024 * (kc + 1)],
                                          dr[wname].ap()[128 * kc:128 * (kc + 1), :]),
                              "dG", 16)
                    for oc in range(NCH):
                        for ts in range(NQT):
                            sl_ = slice(QT * ts, QT * (ts + 1))
                            ps_ = pb0[:, 512 * (ts % 2):512 * (ts % 2 + 1)]
                            p.wait(T, "t", "dG")
                            p.wait(T, "t", "pS")
                            p.wait(T, "t", "pV")
                            for kc in range(NCH):
                                ins = T.matmul(
                                    ps_, wbf[:, 1024 * kc + 128 * oc:
                                             1024 * kc + 128 * (oc + 1)],
                                    hch[kc][:, sl_],
                                    start=(kc == 0), stop=(kc == NCH - 1))
                            p.inc(ins, "pT")
                            p.wait(SC, "s", "pT")
                            p.inc(SC.copy(qraw[:], ps_), "pS")
                            ps2 = pb1[:, 512 * (ts % 2):512 * (ts % 2 + 1)]
                            p.wait(T, "t", "pS")
                            p.inc(T.matmul(ps2, c_ptab[:], qraw[:],
                                           start=True, stop=True), "pT")
                            p.wait(SC, "s", "pT")
                            p.inc(SC.copy(ptmp[:], ps2), "pS")
                            p.wait(V, "v", "pS")
                            p.wait(V, "v", "pV")
                            p.inc(V.tensor_tensor(t1f[:], qraw[:], costab[:, sl_],
                                                  ALU.mult), "pV")
                            p.inc(V.tensor_tensor(t2f[:], ptmp[:], sintab[:, sl_],
                                                  ALU.mult), "pV")
                            p.wait(V, "v", "pV")
                            p.inc(V.tensor_tensor(dst[oc][:, sl_], t1f[:], t2f[:],
                                                  ALU.add), "pV")

            with ExitStack() as en:
                qn = en.enter_context(nc.sbuf_tensor("qn", [2, 512], F32))
                kn = en.enter_context(nc.sbuf_tensor("kn", [2, 512], F32))
                scr2 = en.enter_context(nc.sbuf_tensor("scr2", [2, 512], F32))
                mrowF = en.enter_context(nc.sbuf_tensor("mrowF", [2, S], F32))
                lrow2 = en.enter_context(nc.sbuf_tensor("lrow2", [2, S], BF16))
                sqb = en.enter_context(nc.sbuf_tensor("sqb", [128, QT], BF16))
                g2 = en.enter_context(nc.sbuf_tensor("g2", [2, 8], F32))
                for d in range(NDUO):
                    for ts in range(NQT):
                        sl_ = slice(QT * ts, QT * (ts + 1))
                        p.wait(V, "v", "pT")
                        p.wait(V, "v", "pS")
                        p.inc(V.tensor_tensor(sqb[:], qf[d][:, sl_], qf[d][:, sl_],
                                              ALU.mult), "pV")
                        p.wait(T, "t", "pV")
                        p.wait(T, "t", "pS")
                        T.matmul(pz[0:2, :], c_selE[:], sqb[:], start=True,
                                 stop=True)
                        p.inc(T.matmul(pv0[0:2, :], c_selO[:], sqb[:], start=True,
                                       stop=True), "pT")
                        p.wait(SC, "s", "pT")
                        p.inc(SC.copy(scr2[:], pz[0:2, :]), "pS")
                        p.wait(V, "v", "pS")
                        p.inc(V.tensor_tensor(qn[:], scr2[:], pv0[0:2, :], ALU.max),
                              "pV")
                        p.wait(SC, "s", "pV")
                        p.inc(SC.activation(qn[:], qn[:], AF.Sqrt), "pS")
                        p.wait(V, "v", "pS")
                        p.inc(V.reduce_max(g2[0:2, ts:ts + 1], qn[:], axis=AX.X),
                              "pV")
                    p.wait(V, "v", "pV")
                    p.inc(V.reduce_max(g2[0:2, 4:5], g2[0:2, 0:4], axis=AX.X), "pV")
                    p.wait(V, "v", "pV")
                    p.inc(V.tensor_scalar_mul(g2[0:2, 4:5], g2[0:2, 4:5],
                                              float(SCAL)), "pV")
                    for ts in range(NQT):
                        sl_ = slice(QT * ts, QT * (ts + 1))
                        p.wait(V, "v", "pT")
                        p.wait(V, "v", "pS")
                        p.inc(V.tensor_tensor(sqb[:], kf[d][:, sl_], kf[d][:, sl_],
                                              ALU.mult), "pV")
                        p.wait(T, "t", "pV")
                        p.wait(T, "t", "pS")
                        T.matmul(pz[0:2, :], c_selE[:], sqb[:], start=True,
                                 stop=True)
                        p.inc(T.matmul(pv0[0:2, :], c_selO[:], sqb[:], start=True,
                                       stop=True), "pT")
                        p.wait(SC, "s", "pT")
                        p.inc(SC.copy(scr2[:], pz[0:2, :]), "pS")
                        p.wait(V, "v", "pS")
                        p.inc(V.tensor_tensor(kn[:], scr2[:], pv0[0:2, :], ALU.max),
                              "pV")
                        p.wait(SC, "s", "pV")
                        p.inc(SC.activation(kn[:], kn[:], AF.Sqrt), "pS")
                        p.wait(V, "v", "pS")
                        p.inc(V.tensor_scalar(mrowF[0:2, sl_], kn[:], g2[0:2, 4:5],
                                              None, ALU.mult), "pV")
                    p.wait(V, "v", "pV")
                    p.inc(V.reduce_max(g2[0:2, 5:6], mrowF[:], axis=AX.X), "pV")
                    p.wait(V, "v", "pV")
                    p.inc(V.tensor_scalar_mul(g2[0:2, 5:6], g2[0:2, 5:6], -1.0),
                          "pV")
                    p.wait(SC, "s", "pV")
                    p.inc(SC.activation(lrow2[:], mrowF[:], AF.Exp,
                                        bias=g2[0:2, 5:6]), "pS")
                    p.wait(T, "t", "pV")
                    p.wait(T, "t", "pS")
                    for kcc in range(NKC):
                        T.matmul(pm[:, 2 * kcc:2 * kcc + 2],
                                 mrowF[0:2, 128 * kcc:128 * (kcc + 1)],
                                 c_nid2[0:2, 0:2], start=True, stop=True)
                        ins = T.matmul(pz[:, 2 * kcc:2 * kcc + 2],
                                       lrow2[0:2, 128 * kcc:128 * (kcc + 1)],
                                       c_idb[0:2, 0:2], start=True, stop=True)
                    p.inc(ins, "pT")
                    p.wait(SC, "s", "pT")
                    SC.copy(mT[d][:], pm[:, 0:32])
                    SC.copy(lT[d][:], pz[:, 0:32])
                    p.inc(SC.copy(lTf[d][:], pz[:, 0:32]), "pS")
                    p.wait(T, "t", "pS")
                    p.wait(V, "v", "pS")

            p.wait(G, "g", "pT")
            for kc in range(NCH):
                p.wait(G, "g", "dG")
                p.inc(G.dma_start(wbf[:, 1024 * kc:1024 * (kc + 1)],
                                  dr["wv"].ap()[128 * kc:128 * (kc + 1), :]),
                      "dG", 16)
            for tc in range(NKC):
                for hf in range(2):
                    ps_ = pb0[:, 512 * hf:512 * (hf + 1)]
                    p.wait(T, "t", "dG")
                    p.wait(T, "t", "pS")
                    for kc in range(NCH):
                        ins = T.matmul(
                            ps_, hch[kc][:, 128 * tc:128 * (tc + 1)],
                            wbf[:, 1024 * kc + 512 * hf:1024 * kc + 512 * (hf + 1)],
                            start=(kc == 0), stop=(kc == NCH - 1))
                    p.inc(ins, "pT")
                    p.wait(SC, "s", "pT")
                    p.inc(SC.copy(vt[tc][:, 512 * hf:512 * (hf + 1)], ps_), "pS")
                p.wait(V, "v", "pS")
                for gp in range(NPAIR):
                    ins = V.tensor_scalar(
                        vt[tc][:, 64 * gp:64 * (gp + 1)],
                        vt[tc][:, 64 * gp:64 * (gp + 1)],
                        lTf[gp // 2][:, 2 * tc + gp % 2:2 * tc + gp % 2 + 1],
                        None, ALU.mult)
                p.inc(ins, "pV")

        p.barrier(ENGS)

        if STOP_AFTER == "qkv":
            G.wait_ge(p.h["dG"], p.c["dG"])
            return
        with ExitStack() as ea:
            q = lambda n, s_, d_: ea.enter_context(nc.sbuf_tensor(n, s_, d_))
            eAb = q("eAb", [128, 1024], BF16)
            eBb = q("eBb", [128, 1024], BF16)
            attnq = [q(f"attnq{i}", [128, QT], BF16) for i in range(NCH)]
            xt = [q(f"xt{i}", [128, QT], F32) for i in range(NCH)]
            zr = q("zr", [128, 512], F32)
            zgp = q("zgp", [128, 512], F32)
            As = [q(f"As{i}", [64, 512], F32) for i in range(4)]
            tt0 = q("tt0", [64, 512], F32)
            tt1 = q("tt1", [64, 512], F32)
            anf = q("anf", [64, 512], F32)
            ansq = q("ansq", [64, 512], BF16)
            r2a = q("r2a", [1, 512], F32)
            rmsb = q("rmsb", [1, 512], BF16)
            woW = [q(f"woW{i}", [128, 1024], BF16) for i in range(NCH)]
            wtmp = q("wtmp", [128, 1024], F32)
            t4 = q("t4", [128, 512], F32)

            for kc in range(NCH):
                p.wait(SY, "y", "pS")
                p.wait(SY, "y", "dS")
                p.inc(SY.dma_start(wtmp[:], dr["wo"].ap()[128 * kc:128 * (kc + 1), :]),
                      "dS", 16)
                p.wait(SC, "s", "dS")
                p.inc(SC.activation(woW[kc][:], wtmp[:], AF.Copy,
                                    scale=gpcol[:, 0:1]), "pS")

            for qt in range(NQT):
                p.wait(SY, "y", "pV")
                p.wait(SY, "y", "dS")
                for ci in range(NCH):
                    p.wait(SY, "y", "dS")
                    p.inc(SY.dma_start(xt[ci][:],
                                       dr["x"].ap()[128 * ci:128 * (ci + 1),
                                                    QT * qt:QT * (qt + 1)]),
                          "dS", 16)
                for d in range(NDUO):
                    p.wait(V, "v", "pT")
                    p.wait(V, "v", "pS")
                    p.wait(V, "v", "pV")
                    p.inc(V.memset(pz[:], 1.0), "pV")
                    p.wait(T, "t", "pV")
                    for kt in range(NKC):
                        p.wait(T, "t", "pS")
                        T.matmul(pb0[:, 0:512], kf[d][0:32, 128 * kt:128 * (kt + 1)],
                                 qf[d][0:32, QT * qt:QT * (qt + 1)],
                                 start=True, stop=True, tile_position=(0, 0))
                        T.matmul(pb0[:, 512:1024],
                                 kf[d][32:64, 128 * kt:128 * (kt + 1)],
                                 qf[d][32:64, QT * qt:QT * (qt + 1)],
                                 start=True, stop=True, tile_position=(32, 0))
                        T.matmul(pb1[:, 0:512], kf[d][64:96, 128 * kt:128 * (kt + 1)],
                                 qf[d][64:96, QT * qt:QT * (qt + 1)],
                                 start=True, stop=True, tile_position=(64, 0))
                        p.inc(T.matmul(pb1[:, 512:1024],
                                       kf[d][96:128, 128 * kt:128 * (kt + 1)],
                                       qf[d][96:128, QT * qt:QT * (qt + 1)],
                                       start=True, stop=True,
                                       tile_position=(96, 0)), "pT")
                        p.wait(SC, "s", "pT")
                        SC.activation(eAb[:], pb0[:], AF.Exp, scale=float(SCAL),
                                      bias=mT[d][:, 2 * kt:2 * kt + 1])
                        p.inc(SC.activation(eBb[:], pb1[:], AF.Exp, scale=float(SCAL),
                                            bias=mT[d][:, 2 * kt + 1:2 * kt + 2]),
                              "pS")
                        p.wait(T, "t", "pS")
                        first = (kt == 0)
                        last = (kt == NKC - 1)
                        T.matmul(pz[0:1, :], lT[d][:, 2 * kt:2 * kt + 1],
                                 eAb[:, 0:512], start=first, stop=last,
                                 tile_position=(0, 0))
                        T.matmul(pz[32:33, :], lT[d][:, 2 * kt:2 * kt + 1],
                                 eAb[:, 512:1024], start=first, stop=last,
                                 tile_position=(0, 32), skip_group_check=True)
                        T.matmul(pz[64:65, :], lT[d][:, 2 * kt + 1:2 * kt + 2],
                                 eBb[:, 0:512], start=first, stop=last,
                                 tile_position=(0, 64), skip_group_check=True)
                        T.matmul(pz[96:97, :], lT[d][:, 2 * kt + 1:2 * kt + 2],
                                 eBb[:, 512:1024], start=first, stop=last,
                                 tile_position=(0, 96), skip_group_check=True)
                        T.matmul(pv0[0:64, :], vt[kt][:, 128 * d:128 * d + 64],
                                 eAb[:, 0:512], start=first, stop=last,
                                 tile_position=(0, 0))
                        T.matmul(pv0[64:128, :], vt[kt][:, 128 * d:128 * d + 64],
                                 eAb[:, 512:1024], start=first, stop=last,
                                 tile_position=(0, 64), skip_group_check=True)
                        T.matmul(pv1[0:64, :], vt[kt][:, 128 * d + 64:128 * (d + 1)],
                                 eBb[:, 0:512], start=first, stop=last,
                                 tile_position=(0, 0))
                        p.inc(T.matmul(pv1[64:128, :],
                                       vt[kt][:, 128 * d + 64:128 * (d + 1)],
                                       eBb[:, 512:1024], start=first, stop=last,
                                       tile_position=(0, 64),
                                       skip_group_check=True), "pT")
                    p.wait(V, "v", "pT")
                    p.inc(V.reciprocal(zr[:], pz[:]), "pV")
                    p.wait(V, "v", "pV")
                    for r in (0, 32, 64, 96):
                        p.inc(V.tensor_copy(zgp[r:r + 1, :], zr[r:r + 1, :]), "pV")
                    p.wait(SC, "s", "pT")
                    SC.copy(As[0][:], pv0[0:64, :])
                    SC.copy(As[1][:], pv0[64:128, :])
                    SC.copy(As[2][:], pv1[0:64, :])
                    p.inc(SC.copy(As[3][:], pv1[64:128, :]), "pS")
                    p.wait(T, "t", "pS")
                    p.wait(T, "t", "pV")
                    ub = [pv0[0:64, :], pv1[0:64, :], pz[0:64, :], pm[0:64, :]]
                    T.matmul(ub[0], oneslam[0:1, :], zgp[0:1, :], start=True,
                             stop=True, tile_position=(0, 0))
                    T.matmul(ub[1], oneslam[32:33, :], zgp[32:33, :], start=True,
                             stop=True, tile_position=(32, 0))
                    T.matmul(ub[2], oneslam[64:65, :], zgp[64:65, :], start=True,
                             stop=True, tile_position=(64, 0))
                    p.inc(T.matmul(ub[3], oneslam[96:97, :], zgp[96:97, :],
                                   start=True, stop=True, tile_position=(96, 0)),
                          "pT")
                    for pl in range(2):
                        p.wait(V, "v", "pT")
                        p.wait(V, "v", "pV")
                        p.inc(V.tensor_tensor(tt0[:], As[2 * pl][:], ub[2 * pl],
                                              ALU.mult), "pV")
                        p.inc(V.tensor_tensor(tt1[:], As[2 * pl + 1][:],
                                              ub[2 * pl + 1], ALU.mult), "pV")
                        p.wait(V, "v", "pV")
                        p.inc(V.tensor_sub(anf[:], tt0[:], tt1[:]), "pV")
                        p.wait(V, "v", "pV")
                        p.inc(V.tensor_tensor(ansq[:], anf[:], anf[:], ALU.mult),
                              "pV")
                        p.wait(T, "t", "pV")
                        p.inc(T.matmul(pb0[0:1, 512 * pl:512 * (pl + 1)],
                                       c_ocb[0:64, 0:1], ansq[:],
                                       start=True, stop=True), "pT")
                        p.wait(V, "v", "pT")
                        p.inc(V.tensor_scalar(r2a[:],
                                              pb0[0:1, 512 * pl:512 * (pl + 1)],
                                              1.0 / 64, EPS, ALU.mult, ALU.add),
                              "pV")
                        p.wait(V, "v", "pV")
                        p.inc(V.reciprocal(r2a[:], r2a[:]), "pV")
                        p.wait(SC, "s", "pV")
                        p.inc(SC.activation(r2a[:], r2a[:], AF.Sqrt), "pS")
                        p.wait(V, "v", "pS")
                        p.inc(V.tensor_copy(rmsb[:], r2a[:]), "pV")
                        p.wait(T, "t", "pV")
                        p.inc(T.matmul(pb1[0:64, 512 * pl:512 * (pl + 1)],
                                       c_orb[0:1, 0:64], rmsb[:],
                                       start=True, stop=True), "pT")
                        p.wait(V, "v", "pT")
                        p.inc(V.tensor_tensor(attnq[d][64 * pl:64 * (pl + 1), :],
                                              anf[:],
                                              pb1[0:64, 512 * pl:512 * (pl + 1)],
                                              ALU.mult), "pV")
                for oc in range(NCH):
                    ps_ = pb0[:, 512 * (oc % 2):512 * (oc % 2 + 1)]
                    p.wait(T, "t", "pV")
                    p.wait(T, "t", "pS")
                    for kc in range(NCH):
                        ins = T.matmul(ps_, woW[kc][:, 128 * oc:128 * (oc + 1)],
                                       attnq[kc][:], start=(kc == 0),
                                       stop=(kc == NCH - 1))
                    p.inc(ins, "pT")
                    p.wait(SC, "s", "pT")
                    p.inc(SC.activation(t4[:], ps_, AF.Copy,
                                        scale=gacol[:, oc:oc + 1]), "pS")
                    p.wait(V, "v", "pS")
                    p.wait(V, "v", "dS")
                    p.inc(V.tensor_tensor(xt[oc][:], xt[oc][:], t4[:], ALU.add),
                          "pV")
                    p.wait(SY, "y", "pV")
                    p.wait(SY, "y", "dS")
                    p.inc(SY.dma_start(dr["x2scr"].ap()[128 * oc:128 * (oc + 1),
                                                        QT * qt:QT * (qt + 1)],
                                       xt[oc][:]), "dS", 16)

    p.barrier(ENGS)

    if STOP_AFTER == "attn":
        G.wait_ge(p.h["dG"], p.c["dG"])
        return
    with ExitStack() as ew:
        q = lambda n, s_, d_: ew.enter_context(nc.sbuf_tensor(n, s_, d_))
        wfp = [q(f"wfp{i}", [128, 4096], F32) for i in range(2)]
        wsum = q("wsum", [128, 1], F32)
        tcol = q("tcol", [128, 1], F32)
        m11 = q("m11", [1, 1], F32)
        hcol = q("hcol", [128, 1], F32)
        nhcol = q("nhcol", [128, 1], F32)
        d1 = q("d1", [128, 4096], F32)
        wqn = q("wqn", [128, 4096], BF16)

        for wi, (src, dst, nrow, ncol) in enumerate((
                ("wg", "wgq", D, INNER), ("wu", "wuq", D, INNER),
                ("wd", "wdq", INNER, D))):
            nchw = nrow // 128
            p.wait(V, "v", "pT")
            p.inc(V.memset(wsum[:], 0.0), "pV")
            for ch in range(nchw):
                buf = wfp[ch % 2]
                p.wait(SY, "y", "pV")
                p.wait(SY, "y", "dS")
                p.inc(SY.dma_start(buf[:, 0:ncol],
                                   dr[src].ap()[128 * ch:128 * (ch + 1), :]),
                      "dS", 16)
                p.wait(V, "v", "dS")
                p.inc(V.reduce_sum(tcol[:], buf[:, 0:ncol], axis=AX.X,
                                   apply_absolute_value=True), "pV")
                p.wait(V, "v", "pV")
                p.inc(V.tensor_add(wsum[:], wsum[:], tcol[:]), "pV")
            p.wait(T, "t", "pV")
            p.wait(T, "t", "pS")
            p.inc(T.matmul(pm[0:1, 0:1], wsum[:], c_ocf[:], start=True, stop=True),
                  "pT")
            p.wait(SC, "s", "pT")
            p.inc(SC.copy(m11[:], pm[0:1, 0:1]), "pS")
            p.wait(V, "v", "pS")
            p.inc(V.tensor_scalar(m11[:], m11[:], 1.0 / (nrow * ncol), 1e-5,
                                  ALU.mult, ALU.max), "pV")
            p.wait(T, "t", "pV")
            p.inc(T.matmul(pm[:, 1:2], c_orf[:], m11[:], start=True, stop=True),
                  "pT")
            p.wait(SC, "s", "pT")
            p.inc(SC.copy(swcol[:, wi:wi + 1], pm[:, 1:2]), "pS")
            p.wait(V, "v", "pS")
            p.inc(V.tensor_scalar_mul(hcol[:], swcol[:, wi:wi + 1], 0.5), "pV")
            p.wait(V, "v", "pV")
            p.inc(V.tensor_scalar_mul(nhcol[:], hcol[:], -1.0), "pV")
            for ch in range(nchw):
                buf = wfp[ch % 2]
                p.wait(SY, "y", "pV")
                p.wait(SY, "y", "dS")
                p.inc(SY.dma_start(buf[:, 0:ncol],
                                   dr[src].ap()[128 * ch:128 * (ch + 1), :]),
                      "dS", 16)
                p.wait(V, "v", "dS")
                p.wait(V, "v", "dG")
                p.inc(V.tensor_scalar(d1[:, 0:ncol], buf[:, 0:ncol], hcol[:, 0:1],
                                      None, ALU.is_gt), "pV")
                p.wait(V, "v", "pV")
                p.inc(V.scalar_tensor_tensor(wqn[:, 0:ncol], buf[:, 0:ncol],
                                             nhcol[:, 0:1], d1[:, 0:ncol],
                                             ALU.is_lt, ALU.subtract), "pV")
                p.wait(G, "g", "pV")
                p.wait(G, "g", "dG")
                p.inc(G.dma_start(dr[dst].ap()[128 * ch:128 * (ch + 1), :],
                                  wqn[:, 0:ncol]), "dG", 16)

    p.barrier(ENGS)

    with ExitStack() as ef:
        q = lambda n, s_, d_: ef.enter_context(nc.sbuf_tensor(n, s_, d_))
        x2t = [q(f"x2t{i}", [128, QT], F32) for i in range(NCH)]
        h2t = [q(f"h2t{i}", [128, QT], F32) for i in range(NCH)]
        xqt = [q(f"xqt{i}", [128, QT], BF16) for i in range(NCH)]
        mur2 = q("mur2", [1, QT], F32)
        exr2 = q("exr2", [1, QT], F32)
        rar2 = q("rar2", [1, QT], F32)
        rbr2 = q("rbr2", [1, QT], F32)
        rA2b = q("rA2b", [128, QT], F32)
        rB2b = q("rB2b", [128, QT], F32)
        x2b = q("x2b", [128, QT], BF16)
        x2s = q("x2s", [128, QT], BF16)
        mtok = q("mtok", [128, 4], F32)
        stok = q("stok", [128, 4], F32)
        gsc = q("gsc", [128, 4], F32)
        usc = q("usc", [128, 4], F32)
        srow = q("srow", [1, QT], F32)
        vv5 = q("vv5", [128, 512], F32)
        t32 = q("t32", [128, 512], I32)
        cb1 = q("cb1", [128, 512], F32)
        cb2 = q("cb2", [128, 512], F32)
        rr5 = q("rr5", [128, 512], F32)
        wgb = q("wgb", [128, 4096], BF16)
        wub = q("wub", [128, 4096], BF16)
        gsil = q("gsil", [128, 512], F32)
        prod = [q(f"prod{i}", [128, 4096], F32) for i in range(4)]
        pq1 = q("pq1", [128, 4096], BF16)
        pqT = [q(f"pqT{i}", [128, 512], BF16) for i in range(NIC)]
        m2c = q("m2c", [128, 4], F32)
        s2c = q("s2c", [128, 4], F32)
        gbd = q("gbd", [128, 8], F32)
        wdb = q("wdb", [128, 4096], BF16)
        t5 = q("t5", [128, 512], F32)
        t6 = q("t6", [128, 512], F32)

        p.wait(V, "v", "pS")
        for oc in range(NCH):
            p.inc(V.tensor_mul(gbd[:, oc:oc + 1], gacol[:, 8 + oc:9 + oc],
                               swcol[:, 2:3]), "pV")
            p.wait(V, "v", "pV")
            p.inc(V.tensor_scalar_mul(gbd[:, oc:oc + 1], gbd[:, oc:oc + 1],
                                      -1.0), "pV")

        def round_to(dst_ap, src_ap, n):
            p.wait(V, "v", "pV")
            p.inc(V.tensor_copy(t32[:, 0:n], src_ap), "pV")
            p.wait(V, "v", "pV")
            p.inc(V.scalar_tensor_tensor(cb1[:, 0:n], t32[:, 0:n], 0.5, src_ap,
                                         ALU.add, ALU.is_le), "pV")
            p.inc(V.scalar_tensor_tensor(cb2[:, 0:n], t32[:, 0:n], -0.5, src_ap,
                                         ALU.add, ALU.is_ge), "pV")
            p.wait(V, "v", "pV")
            p.inc(V.tensor_tensor(rr5[:, 0:n], t32[:, 0:n], cb1[:, 0:n], ALU.add),
                  "pV")
            p.wait(V, "v", "pV")
            p.inc(V.tensor_sub(dst_ap, rr5[:, 0:n], cb2[:, 0:n]), "pV")

        for tt in range(NQT):
            p.wait(SY, "y", "pV")
            p.wait(SY, "y", "dS")
            for ci in range(NCH):
                p.wait(SY, "y", "dS")
                p.inc(SY.dma_start(x2t[ci][:],
                                   dr["x2scr"].ap()[128 * ci:128 * (ci + 1),
                                                    QT * tt:QT * (tt + 1)]),
                      "dS", 16)
            for ci in range(NCH):
                p.wait(V, "v", "dS")
                p.wait(V, "v", "pT")
                p.inc(V.tensor_copy(x2b[:], x2t[ci][:]), "pV")
                p.inc(V.tensor_tensor(x2s[:], x2t[ci][:], x2t[ci][:], ALU.mult), "pV")
                p.wait(T, "t", "pV")
                p.wait(T, "t", "pS")
                T.matmul(pz[0:1, :], c_ocb[:], x2b[:], start=(ci == 0),
                         stop=(ci == NCH - 1))
                p.inc(T.matmul(pv0[0:1, :], c_ocb[:], x2s[:], start=(ci == 0),
                               stop=(ci == NCH - 1)), "pT")
            p.wait(SC, "s", "pT")
            SC.copy(mur2[:], pz[0:1, :])
            p.inc(SC.copy(exr2[:], pv0[0:1, :]), "pS")
            p.wait(V, "v", "pS")
            p.inc(V.tensor_scalar_mul(mur2[:], mur2[:], 1.0 / D), "pV")
            p.inc(V.tensor_scalar_mul(exr2[:], exr2[:], 1.0 / D), "pV")
            p.wait(V, "v", "pV")
            p.inc(V.tensor_tensor(rar2[:], mur2[:], mur2[:], ALU.mult), "pV")
            p.wait(V, "v", "pV")
            p.inc(V.tensor_sub(rar2[:], exr2[:], rar2[:]), "pV")
            p.wait(V, "v", "pV")
            p.inc(V.tensor_scalar_add(rar2[:], rar2[:], EPS), "pV")
            p.wait(V, "v", "pV")
            p.inc(V.reciprocal(rar2[:], rar2[:]), "pV")
            p.wait(SC, "s", "pV")
            p.inc(SC.activation(rar2[:], rar2[:], AF.Sqrt), "pS")
            p.wait(V, "v", "pS")
            p.inc(V.scalar_tensor_tensor(rbr2[:], mur2[:], -1.0, rar2[:],
                                         ALU.mult, ALU.mult), "pV")
            p.wait(T, "t", "pV")
            p.wait(T, "t", "pS")
            p.inc(T.matmul(pz[:, :], c_orf[:], rar2[:], start=True, stop=True),
                  "pT")
            p.wait(SC, "s", "pT")
            p.inc(SC.copy(rA2b[:], pz[:, :]), "pS")
            p.wait(T, "t", "pS")
            p.inc(T.matmul(pz[:, :], c_orf[:], rbr2[:], start=True, stop=True),
                  "pT")
            p.wait(SC, "s", "pT")
            p.inc(SC.copy(rB2b[:], pz[:, :]), "pS")
            for ci in range(NCH):
                p.wait(V, "v", "pS")
                p.wait(V, "v", "pV")
                p.inc(V.tensor_tensor(vv5[:], x2t[ci][:], rA2b[:], ALU.mult), "pV")
                p.wait(V, "v", "pV")
                p.inc(V.tensor_tensor(vv5[:], vv5[:], rB2b[:], ALU.add), "pV")
                p.wait(SC, "s", "pV")
                p.inc(SC.activation(h2t[ci][:], vv5[:], AF.Identity,
                                    scale=capl1[:, 8 + ci:9 + ci],
                                    bias=modc[:, 24 + ci:25 + ci]), "pS")
            p.wait(V, "v", "pS")
            p.wait(V, "v", "pV")
            p.inc(V.tensor_scalar_mul(cb1[:], h2t[0][:], -1.0), "pV")
            p.wait(V, "v", "pV")
            p.inc(V.tensor_tensor(cb2[:], cb1[:], h2t[0][:], ALU.max), "pV")
            for ci in range(1, NCH):
                p.wait(V, "v", "pV")
                p.inc(V.tensor_scalar_mul(cb1[:], h2t[ci][:], -1.0), "pV")
                p.wait(V, "v", "pV")
                p.inc(V.tensor_tensor(cb1[:], cb1[:], h2t[ci][:], ALU.max), "pV")
                p.wait(V, "v", "pV")
                p.inc(V.tensor_tensor(cb2[:], cb2[:], cb1[:], ALU.max), "pV")
            p.wait(T, "t", "pV")
            p.wait(T, "t", "pS")
            for j in range(4):
                p.inc(T.matmul(pm[:, 128 * j:128 * (j + 1)],
                               cb2[:, 128 * j:128 * (j + 1)], c_idf[:],
                               start=True, stop=True), "pT")
            p.wait(V, "v", "pT")
            for j in range(4):
                p.inc(V.reduce_max(mtok[:, j:j + 1], pm[:, 128 * j:128 * (j + 1)],
                                   axis=AX.X), "pV")
            p.wait(V, "v", "pV")
            p.inc(V.tensor_scalar_max(mtok[:], mtok[:], 1e-5), "pV")
            p.wait(V, "v", "pV")
            p.inc(V.reciprocal(stok[:], mtok[:]), "pV")
            p.wait(V, "v", "pV")
            p.inc(V.tensor_scalar_mul(stok[:], stok[:], 127.0), "pV")
            p.wait(V, "v", "pV")
            for j in range(4):
                p.inc(V.tensor_mul(gsc[:, j:j + 1], mtok[:, j:j + 1], swcol[:, 0:1]),
                      "pV")
                p.inc(V.tensor_mul(usc[:, j:j + 1], mtok[:, j:j + 1], swcol[:, 1:2]),
                      "pV")
            p.wait(V, "v", "pV")
            p.inc(V.tensor_scalar_mul(gsc[:], gsc[:], -1.0 / 127.0), "pV")
            p.inc(V.tensor_scalar_mul(usc[:], usc[:], -1.0 / 127.0), "pV")
            p.wait(T, "t", "pV")
            p.wait(T, "t", "pS")
            for j in range(4):
                p.inc(T.matmul(pm[0:1, 128 * j:128 * (j + 1)], stok[:, j:j + 1],
                               c_idf[:], start=True, stop=True), "pT")
            p.wait(SC, "s", "pT")
            p.inc(SC.copy(srow[:], pm[0:1, 0:512]), "pS")
            p.wait(T, "t", "pS")
            p.inc(T.matmul(pz[:, :], c_orf[:], srow[:], start=True, stop=True), "pT")
            for ci in range(NCH):
                p.wait(V, "v", "pT")
                p.wait(V, "v", "pV")
                p.inc(V.tensor_tensor(vv5[:], h2t[ci][:], pz[:, :], ALU.mult), "pV")
                round_to(xqt[ci][:], vv5[:], 512)
            for isl in range(8):
                p.wait(G, "g", "pT")
                for kc in range(NCH):
                    p.wait(G, "g", "dG")
                    p.inc(G.dma_start(wgb[:, 512 * kc:512 * (kc + 1)],
                                      dr["wgq"].ap()[128 * kc:128 * (kc + 1),
                                                     512 * isl:512 * (isl + 1)]),
                          "dG", 16)
                    p.wait(G, "g", "dG")
                    p.inc(G.dma_start(wub[:, 512 * kc:512 * (kc + 1)],
                                      dr["wuq"].ap()[128 * kc:128 * (kc + 1),
                                                     512 * isl:512 * (isl + 1)]),
                          "dG", 16)
                for tc in range(4):
                    gp_ = pb0[:, 512 * (tc % 2):512 * (tc % 2 + 1)]
                    up_ = pb1[:, 512 * (tc % 2):512 * (tc % 2 + 1)]
                    p.wait(T, "t", "dG")
                    p.wait(T, "t", "pS")
                    p.wait(T, "t", "pV")
                    for kc in range(NCH):
                        T.matmul(gp_, xqt[kc][:, 128 * tc:128 * (tc + 1)],
                                 wgb[:, 512 * kc:512 * (kc + 1)],
                                 start=(kc == 0), stop=(kc == NCH - 1))
                        ins = T.matmul(up_, xqt[kc][:, 128 * tc:128 * (tc + 1)],
                                       wub[:, 512 * kc:512 * (kc + 1)],
                                       start=(kc == 0), stop=(kc == NCH - 1))
                    p.inc(ins, "pT")
                    p.wait(SC, "s", "pT")
                    p.inc(SC.activation(gsil[:], gp_, AF.Silu,
                                        scale=gsc[:, tc:tc + 1]), "pS")
                    p.wait(V, "v", "pS")
                    p.inc(V.scalar_tensor_tensor(
                        prod[tc][:, 512 * isl:512 * (isl + 1)], gsil[:],
                        usc[:, tc:tc + 1], up_, ALU.mult, ALU.mult), "pV")
            for tc in range(4):
                p.wait(V, "v", "pV")
                p.inc(V.reduce_max(m2c[:, tc:tc + 1], prod[tc][:], axis=AX.X,
                                   apply_absolute_value=True), "pV")
                p.wait(V, "v", "pV")
                p.inc(V.tensor_scalar_max(m2c[:, tc:tc + 1], m2c[:, tc:tc + 1],
                                          1e-5), "pV")
                p.wait(V, "v", "pV")
                p.inc(V.reciprocal(s2c[:, tc:tc + 1], m2c[:, tc:tc + 1]), "pV")
                p.wait(V, "v", "pV")
                p.inc(V.tensor_scalar_mul(s2c[:, tc:tc + 1], s2c[:, tc:tc + 1],
                                          127.0), "pV")
                for sj in range(8):
                    ssl = slice(512 * sj, 512 * (sj + 1))
                    p.wait(V, "v", "pV")
                    p.wait(V, "v", "pT")
                    p.inc(V.tensor_scalar(vv5[:], prod[tc][:, ssl],
                                          s2c[:, tc:tc + 1], None, ALU.mult), "pV")
                    round_to(pq1[:, ssl], vv5[:], 512)
                for ic4 in range(NIC // 4):
                    p.wait(T, "t", "pV")
                    p.wait(T, "t", "pS")
                    for j in range(4):
                        ic = 4 * ic4 + j
                        ins = T.matmul(pm[:, 128 * j:128 * (j + 1)],
                                       pq1[:, 128 * ic:128 * (ic + 1)], c_idb[:],
                                       start=True, stop=True)
                    p.inc(ins, "pT")
                    p.wait(SC, "s", "pT")
                    for j in range(4):
                        ic = 4 * ic4 + j
                        ins = SC.copy(pqT[ic][:, 128 * tc:128 * (tc + 1)],
                                      pm[:, 128 * j:128 * (j + 1)])
                    p.inc(ins, "pS")
            p.wait(V, "v", "pV")
            p.inc(V.reciprocal(s2c[:], s2c[:]), "pV")
            p.wait(T, "t", "pV")
            p.wait(T, "t", "pS")
            for j in range(4):
                p.inc(T.matmul(pm[0:1, 128 * j:128 * (j + 1)], s2c[:, j:j + 1],
                               c_idf[:], start=True, stop=True), "pT")
            p.wait(SC, "s", "pT")
            p.inc(SC.copy(srow[:], pm[0:1, 0:512]), "pS")
            p.wait(T, "t", "pS")
            p.inc(T.matmul(pv1[:, :], c_orf[:], srow[:], start=True, stop=True),
                  "pT")
            for oc in range(NCH):
                p.wait(G, "g", "pT")
                src_dma = bass.AP(dr["wdq"], 128 * oc,
                                  [[D, 128], [D * 128, NIC], [1, 128]])
                dst_dma = bass.AP(wdb, 0, [[4096, 128], [128, NIC], [1, 128]])
                p.wait(G, "g", "dG")
                p.inc(G.dma_start(dst_dma, src_dma), "dG", 16)
                p.wait(T, "t", "dG")
                p.wait(T, "t", "pS")
                p.wait(T, "t", "pV")
                for ic in range(NIC):
                    ins = T.matmul(pz[:, :], wdb[:, 128 * ic:128 * (ic + 1)],
                                   pqT[ic][:], start=(ic == 0), stop=(ic == NIC - 1))
                p.inc(ins, "pT")
                p.wait(SC, "s", "pT")
                p.inc(SC.activation(t5[:], pz[:, :], AF.Copy,
                                    scale=gbd[:, oc:oc + 1]), "pS")
                p.wait(V, "v", "pS")
                p.wait(V, "v", "pV")
                p.inc(V.tensor_tensor(t6[:], t5[:], pv1[:, :], ALU.mult), "pV")
                p.wait(V, "v", "pV")
                p.inc(V.tensor_tensor(t6[:], x2t[oc][:], t6[:], ALU.add), "pV")
                p.wait(G, "g", "pV")
                p.wait(G, "g", "dG")
                p.inc(G.dma_start(dr["out"].ap()[128 * oc:128 * (oc + 1),
                                                 QT * tt:QT * (tt + 1)], t6[:]),
                      "dG", 16)
    G.wait_ge(p.h["dG"], p.c["dG"])
    p.barrier(ENGS)




_CONSTS = None


def kernel(L0_attn_x, L0_attn_Wq, L0_attn_Wk, L0_attn_Wv, L0_attn_Wo,
           L0_attn_lambda_q1, L0_attn_lambda_k1, L0_attn_lambda_q2,
           L0_attn_lambda_k2, L0_attn_subln_g, L0_ffn_w_gate, L0_ffn_w_up,
           L0_ffn_w_down, L0_c, L0_wmod):
    global _CONSTS
    if _CONSTS is None:
        _CONSTS = _host_constants()
    f32 = lambda a: np.ascontiguousarray(np.asarray(a), dtype=np.float32)
    xs = np.ascontiguousarray(np.transpose(f32(L0_attn_x), (0, 2, 1)))
    shared = {
        "wq": f32(L0_attn_Wq), "wk": f32(L0_attn_Wk), "wv": f32(L0_attn_Wv),
        "wo": f32(L0_attn_Wo), "wmod": f32(L0_wmod),
        "wg": np.ascontiguousarray(f32(L0_ffn_w_gate).T),
        "wu": np.ascontiguousarray(f32(L0_ffn_w_up).T),
        "wd": np.ascontiguousarray(f32(L0_ffn_w_down).T),
        "lq1": f32(L0_attn_lambda_q1).reshape(1, HD),
        "lk1": f32(L0_attn_lambda_k1).reshape(1, HD),
        "lq2": f32(L0_attn_lambda_q2).reshape(1, HD),
        "lk2": f32(L0_attn_lambda_k2).reshape(1, HD),
        "subg": np.ascontiguousarray(np.tile(f32(L0_attn_subln_g), 2).reshape(128, 1)),
    }
    shared.update(_CONSTS)
    cvecs = f32(L0_c)
    nc = build_nc()
    in_maps = []
    for b in range(8):
        m = dict(shared)
        m["x"] = xs[b]
        m["cvec"] = np.ascontiguousarray(cvecs[b].reshape(D, 1))
        in_maps.append(m)
    res = run_bass_kernel_spmd(nc, in_maps, core_ids=list(range(8)))
    out = np.stack([np.ascontiguousarray(np.asarray(r["out"]).T)
                    for r in res.results])
    return np.ascontiguousarray(out.astype(np.float32))
